# version 08
# speedup vs baseline: 1.5026x; 1.0170x over previous
import math
import numpy as np
import ml_dtypes
from contextlib import ExitStack

import concourse.bass as bass
import concourse.mybir as mybir
from concourse.bass_utils import run_bass_kernel_spmd

F32 = mybir.dt.float32
BF16 = mybir.dt.bfloat16
AF = mybir.ActivationFunctionType
ALU = mybir.AluOpType
AX = mybir.AxisListType
BF = ml_dtypes.bfloat16

T = 2048
D = 1024
NC = 8
DT = D // 128
NCH = T // 512
TF = 2052

NEG = -1e30

MODE = "fused"
_DBG_FFN = None
_DBG_MOE = None
PHASES = ["ln0", "attn0", "ffn0a", "ffn0b", "resid_ln1", "moe", "attn1"]



_CONST_CACHE = {}

def _host_consts():
    if _CONST_CACHE:
        return _CONST_CACHE
    c = {}
    c["c_idb"] = np.eye(128, dtype=np.float32).astype(BF)
    c["c_idf"] = np.eye(128, dtype=np.float32)
    ii = np.arange(128)
    c["c_triqk"] = np.where(ii[None, :] > ii[:, None], NEG, 0.0).astype(np.float32)
    c["c_trikq"] = np.where(ii[:, None] > ii[None, :], NEG, 0.0).astype(np.float32)
    posn = np.arange(T, dtype=np.float32)
    c["c_negphi"] = (-(np.floor(posn / 8.0)))[None, :].astype(np.float32)
    c["c_negplo"] = (-(posn % 8.0))[None, :].astype(np.float32)
    c["c_onesrowf"] = np.ones((1, T), dtype=np.float32)
    c["c_onesrow"] = np.ones((1, T), dtype=np.float32).astype(BF)
    c["c_ones128f"] = np.ones((1, 128), dtype=np.float32)
    c["c_onescolb"] = np.ones((128, 1), dtype=np.float32).astype(BF)
    c["c_ones16"] = np.ones((16, 128), dtype=np.float32).astype(BF)
    perm = np.zeros((128, 128), dtype=np.float32)
    for k in range(128):
        m = (k // 64) * 64 + ((k % 64) + 32) % 64
        perm[k, m] = 1.0
    c["c_permrot"] = perm.astype(BF)
    f = np.arange(1024)
    heads = f // 64
    i16 = np.zeros((16, 1024), dtype=np.float32)
    i16[heads, f] = 1.0
    c["c_i16"] = i16.astype(BF)
    i16s = np.zeros((16, 1024), dtype=np.float32)
    sign = np.where(f % 2 == 0, -1.0, 1.0)
    i16s[heads, f] = sign
    c["c_i16s"] = i16s.astype(BF)
    DH, WIN = 64, 512
    half = np.arange(0, DH, 2, dtype=np.float64)
    inv_freq = 1.0 / 10000.0 ** (half / DH)
    t = np.arange(T, dtype=np.float64)
    freqs = t[:, None] * inv_freq[None, :]
    freqs = np.concatenate([freqs, freqs], axis=-1)
    base = (half + 0.4 * DH) / (1.4 * DH)
    power = (t - T // 2) / float(WIN // 2)
    scale = base[None, :] ** power[:, None]
    scale = np.concatenate([scale, scale], axis=-1)
    cos, sin = np.cos(freqs), np.sin(freqs)
    sgn = np.where(np.arange(DH) < 32, -1.0, 1.0)[None, :]
    q1 = 8.0 * (cos * scale).T
    q2 = 8.0 * (sgn.T * (sin * scale).T)
    k1 = (cos / scale).T
    k2 = (sgn.T * (sin / scale).T)
    stack = lambda a: np.concatenate([a, a], axis=0).astype(np.float32).astype(BF)
    c["c_rotq1"], c["c_rotq2"] = stack(q1), stack(q2)
    c["c_rotk1"], c["c_rotk2"] = stack(k1), stack(k2)
    indsum = np.zeros((128, 32), dtype=np.float32)
    indbc = np.zeros((8, 512), dtype=np.float32)
    for tpt in range(4):
        for j in (2 * tpt, 2 * tpt + 1):
            if j % 2 == 0:
                indsum[0:64, tpt * 8 + j] = 1.0
                indbc[j, tpt * 128: tpt * 128 + 64] = 1.0
            else:
                indsum[64:128, tpt * 8 + j] = 1.0
                indbc[j, tpt * 128 + 64: tpt * 128 + 128] = 1.0
    c["c_indsum"] = indsum.astype(BF)
    c["c_indbc"] = indbc.astype(BF)
    c["c_indbcf"] = indbc.copy()
    indbcf16 = np.zeros((16, 1024), dtype=np.float32)
    for i in range(8):
        for j in (2 * i, 2 * i + 1):
            lo = i * 128 + (0 if j % 2 == 0 else 64)
            indbcf16[j, lo:lo + 64] = 1.0
    c["c_indbcf16"] = indbcf16
    _CONST_CACHE.update(c)
    return c


def _host_weights(inp):
    w = {}
    asc = lambda a: np.ascontiguousarray(np.asarray(a, dtype=np.float32))
    w["wq_t"] = asc(np.asarray(inp["L0_attn_Wq"]).T)
    wq_t = w["wq_t"]
    sw = np.arange(1024) ^ 1
    w["wq_ts"] = asc(wq_t[:, sw])
    w["wk_t"] = asc(np.asarray(inp["L0_attn_Wk"]).T)
    w["wv_t"] = asc(np.asarray(inp["L0_attn_Wv"]).T)
    wo_t = asc(np.asarray(inp["L0_attn_Wo"]).T)
    w["wo_th"] = asc(wo_t.reshape(16, 64, 1024).transpose(1, 0, 2).reshape(64, 16 * 1024))
    w["qphase"] = asc(np.asarray(inp["L0_attn_q_phase_angle"]).reshape(16, 1))
    w["alslope"] = asc(np.asarray(inp["L0_attn_alibi_log_slopes"]).reshape(16, 1))
    HID = 2730
    for l in range(2):
        wqkv = np.asarray(inp["L0_ffn_wqkv"][l], dtype=np.float32)
        wqkvh = np.empty_like(wqkv)
        for h8 in range(8):
            wqkvh[:, 192 * h8:192 * h8 + 64] = wqkv[:, 64 * h8:64 * h8 + 64]
            wqkvh[:, 192 * h8 + 64:192 * h8 + 128] = wqkv[:, 512 + 64 * h8:512 + 64 * h8 + 64]
            wqkvh[:, 192 * h8 + 128:192 * h8 + 192] = wqkv[:, 1024 + 64 * h8:1024 + 64 * h8 + 64]
        w[f"wqkv{l}"] = asc(wqkvh)
        ffwo = asc(inp["L0_ffn_wo"][l])
        w[f"wothl{l}"] = asc(ffwo.reshape(8, 64, 1024).transpose(1, 0, 2).reshape(64, 8 * 1024))
        w[f"qs{l}"] = asc(np.tile(np.asarray(inp["L0_ffn_q_scale"])[l], 2).reshape(128, 1))
        w[f"ks{l}"] = asc(np.tile(np.asarray(inp["L0_ffn_k_scale"])[l], 2).reshape(128, 1))
        w[f"lng{l}"] = asc(np.asarray(inp["L0_ffn_ln_g"])[l].reshape(8, 128).T)
        w[f"lnb{l}"] = asc(np.asarray(inp["L0_ffn_ln_b"])[l].reshape(8, 128).T)
        w1 = np.asarray(inp["L0_ffn_w1"])[l]
        b1 = np.asarray(inp["L0_ffn_b1"])[l]
        w1p = np.zeros((1024, 22 * 256), dtype=np.float32)
        b1p = np.zeros((128, 44), dtype=np.float32)
        for j in range(22):
            na = min(128, HID - 128 * j)
            w1p[:, j * 256: j * 256 + na] = w1[:, 128 * j: 128 * j + na]
            w1p[:, j * 256 + 128: j * 256 + 128 + na] = w1[:, HID + 128 * j: HID + 128 * j + na]
            b1p[:na, 2 * j] = b1[128 * j: 128 * j + na]
            b1p[:na, 2 * j + 1] = b1[HID + 128 * j: HID + 128 * j + na]
        w[f"w1p{l}"] = w1p
        w[f"b1p{l}"] = b1p
        w2p = np.zeros((22 * 128, 1024), dtype=np.float32)
        w2p[:HID] = np.asarray(inp["L0_ffn_w2"])[l]
        w[f"w2p{l}"] = w2p
        w[f"b2c{l}"] = asc(np.asarray(inp["L0_ffn_b2"])[l].reshape(8, 128).T)
    w["gate_t"] = asc(np.asarray(inp["L1_ffn_gate_w"]).T)
    w1e = np.transpose(np.asarray(inp["L1_ffn_W1"]), (0, 2, 1))
    w1ep = np.empty_like(w1e)
    for j in range(6):
        w1ep[:, :, j * 256:j * 256 + 128] = w1e[:, :, 128 * j:128 * (j + 1)]
        w1ep[:, :, j * 256 + 128:j * 256 + 256] = w1e[:, :, 768 + 128 * j:768 + 128 * (j + 1)]
    w["w1ep"] = asc(w1ep)
    w["w2e"] = asc(np.transpose(np.asarray(inp["L1_ffn_W2"]), (0, 2, 1)))
    ws1 = np.asarray(inp["L1_ffn_Ws1"]).T
    ws1p = np.empty_like(ws1)
    for j in range(12):
        ws1p[:, j * 256:j * 256 + 128] = ws1[:, 128 * j:128 * (j + 1)]
        ws1p[:, j * 256 + 128:j * 256 + 256] = ws1[:, 1536 + 128 * j:1536 + 128 * (j + 1)]
    w["ws1p"] = asc(ws1p)
    w["ws2_t"] = asc(np.asarray(inp["L1_ffn_Ws2"]).T)
    return w


_PHASE_WEIGHTS = {
    "ln0": [],
    "attn0": ["wq_t", "wq_ts", "wk_t", "wv_t", "wo_th", "qphase", "alslope"],
    "ffn0a": ["wqkv0", "wothl0", "qs0", "ks0", "lng0", "lnb0", "w1p0", "b1p0", "w2p0", "b2c0"],
    "ffn0b": ["wqkv1", "wothl1", "qs1", "ks1", "lng1", "lnb1", "w1p1", "b1p1", "w2p1", "b2c1"],
    "resid_ln1": [],
    "moe": ["gate_t", "w1ep", "w2e", "ws1p", "ws2_t"],
    "attn1": [],
}
_PHASE_CONSTS = {
    "ln0": ["c_ones128f", "c_onescolb"],
    "attn0": ["c_idf", "c_triqk", "c_trikq", "c_negphi", "c_negplo", "c_onesrowf",
              "c_ones128f", "c_onescolb", "c_i16", "c_i16s", "c_ones16"],
    "ffn0a": ["c_idf", "c_triqk", "c_trikq", "c_ones128f", "c_onescolb",
              "c_permrot", "c_rotq1", "c_rotq2", "c_rotk1", "c_rotk2"],
    "ffn0b": None,
    "resid_ln1": ["c_ones128f", "c_onescolb", "c_idb"],
    "moe": ["c_ones128f", "c_onescolb", "c_idf"],
    "attn1": ["c_idb", "c_ones128f", "c_onescolb"],
}
_PHASE_CONSTS["ffn0b"] = _PHASE_CONSTS["ffn0a"]

DRAM_ONLY_CONSTS = {"c_negphi", "c_negplo", "c_onesrowf", "c_i16", "c_i16s", "c_ones16",
                    "c_rotq1", "c_rotq2", "c_rotk1", "c_rotk2"}

_PHASE_STREAMS = {
    "ln0": ("X", "XB", "Y"),
    "attn0": ("X", "XB"),
    "ffn0a": ("Y", "XB"),
    "ffn0b": ("Y", "XB"),
    "resid_ln1": ("X", "XB", "Y"),
    "moe": ("X", "XB"),
    "attn1": ("X", "XB"),
}

_PHASE_STATE = {
    "ln0":       ([],                ["st_x", "st_y", "st_xb"]),
    "attn0":     (["st_x", "st_xb"], ["st_x"]),
    "ffn0a":     (["st_y"],          ["st_y"]),
    "ffn0b":     (["st_y"],          ["st_y"]),
    "resid_ln1": (["st_x", "st_y"],  ["st_x", "st_xb"]),
    "moe":       (["st_x", "st_xb"], ["st_x"]),
    "attn1":     (["st_x", "st_xb"], []),
}



class Sem:
    def __init__(self, h, name):
        self.h = h
        self.name = name
        self.n = 0


ENGS = ("pe", "v", "sc", "gp", "sy")


class Ctx:

    def __init__(self, nc, es, phase_inputs):
        self.nc = nc
        self.es = es
        self.scopes = [es]
        self.prefix = ""
        self.inp = phase_inputs
        self.buf = {}
        self.ops = {e: [] for e in ENGS}
        self.s = {}
        for name in ["pe", "v", "sc", "gp", "dg", "ds", "out"]:
            self.s[name] = Sem(es.enter_context(nc.semaphore(f"s_{name}")), name)

    def op(self, eng, fn, k=1, sem=None):
        s = self.s[eng] if sem is None else sem
        h, kk = s.h, k
        self.ops[eng].append(lambda e: fn(e).then_inc(h, kk))
        s.n += k
        return s.n

    def opn(self, eng, fn):
        self.ops[eng].append(fn)

    def wait(self, eng, sem, n=None):
        h, nn = sem.h, (sem.n if n is None else n)
        self.ops[eng].append(lambda e: e.wait_ge(h, nn))

    def dma(self, q, out_ap, in_ap, slow=False):
        sem = self.s["ds"] if q == "sy" else self.s["dg"]
        return self.op(q, lambda e: e.dma_start(out=out_ap, in_=in_ap,
                                                allow_slow_non_contiguous=slow), 16, sem)

    def barrier(self):
        targets = [(self.s[n], self.s[n].n) for n in ["pe", "v", "sc", "gp", "dg", "ds"]]
        for eng in ENGS:
            for sem, n in targets:
                self.wait(eng, sem, n)

    def finalize(self):
        block = self.es.enter_context(self.nc.Block())
        def run(lst):
            def f(e):
                for fn in lst:
                    fn(e)
            return f
        block.tensor(run(self.ops["pe"]))
        block.vector(run(self.ops["v"]))
        block.scalar(run(self.ops["sc"]))
        block.gpsimd(run(self.ops["gp"]))
        block.sync(run(self.ops["sy"]))

    def push_scope(self):
        es2 = ExitStack()
        self.scopes.append(es2)
        return es2

    def pop_scope(self):
        self.scopes.pop().close()

    def sb(self, name, shape, dt=F32):
        t = self.scopes[-1].enter_context(
            self.nc.sbuf_tensor(self.prefix + name, shape, dt))
        self.buf[name] = t
        return t

    def psum(self, name, shape, dt=F32):
        t = self.scopes[-1].enter_context(
            self.nc.psum_tensor(self.prefix + name, shape, dt))
        self.buf[name] = t
        return t



def mm(ctx, out, lhsT, rhs, start=True, stop=True):
    return ctx.op("pe", lambda e: e.matmul(out, lhsT, rhs, start=start, stop=stop))

def tr(ctx, out, in_, ident):
    return ctx.op("pe", lambda e: e.transpose(out, in_, ident))

def tt(ctx, eng, out, a, b, alu):
    return ctx.op(eng, lambda e: e.tensor_tensor(out, a, b, alu))

def ts(ctx, eng, out, a, s1, s2, op0, op1=None):
    if s2 is None and op1 is None:
        return ctx.op(eng, lambda e: e.tensor_single_scalar(out, a, s1, op0))
    return ctx.op(eng, lambda e: e.tensor_scalar(out, a, s1, s2, op0, op1))

def stt(ctx, eng, out, in0, scalar, in1, op0, op1):
    return ctx.op(eng, lambda e: e.scalar_tensor_tensor(out, in0, scalar, in1, op0, op1))

def cp(ctx, eng, out, in_):
    return ctx.op(eng, lambda e: e.tensor_copy(out, in_))

def scp(ctx, out, in_):
    return ctx.op("sc", lambda e: e.copy(out, in_))

def act(ctx, out, in_, func, bias=0.0, scale=1.0, accum=None):
    return ctx.op("sc", lambda e: e.activation(out, in_, func, bias=bias, scale=scale,
                                               accum_out=accum))

def red(ctx, out, in_, alu=ALU.max, negate=False):
    return ctx.op("v", lambda e: e.tensor_reduce(out, in_, AX.X, alu, negate=negate))

def recip(ctx, out, in_):
    return ctx.op("v", lambda e: e.reciprocal(out, in_))

def mset(ctx, eng, ap_, val):
    return ctx.op(eng, lambda e: e.memset(ap_, val))



def alloc_stream(ctx, name, dt=F32, n=DT):
    return [ctx.sb(f"{name}{i}", [128, TF], dt) for i in range(n)]


def load_stream(ctx, tiles, dram_ap, cast=False):
    for i, t in enumerate(tiles):
        ctx.dma("gp" if cast else "sy", t[:, 1:T + 1], dram_ap[128 * i:128 * (i + 1), :])
    for t in tiles:
        mset(ctx, "gp", t[:, 0:1], 0.0)
        mset(ctx, "gp", t[:, T + 1:TF], 0.0)


def store_stream(ctx, tiles, dram_ap):
    for i, t in enumerate(tiles):
        a, b = dram_ap[128 * i:128 * (i + 1), :], t[:, 1:T + 1]
        ctx.op("sy", lambda e, a=a, b=b: e.dma_start(out=a, in_=b), 16, ctx.s["out"])


def col(c, w=512):
    return slice(1 + w * c, 1 + w * (c + 1))


def col_sh(c, w=512):
    return slice(w * c, w * (c + 1))



def emit_ln_fm(ctx, src_tiles, srcb_tiles, out_tiles, outb_tiles, pref,
               gamma=None, beta=None, shifted=False, ps_pool=None):
    nc = ctx.nc
    spe, sv, ssc = ctx.s["pe"], ctx.s["v"], ctx.s["sc"]
    ones_col = ctx.buf["c_onescolb"]
    ones128f = ctx.buf["c_ones128f"]

    es2 = ExitStack()
    if ps_pool is None:
        ps_pool = es2.enter_context(nc.psum_tensor(f"{pref}_pp", [128, T], F32))
    strip = es2.enter_context(nc.sbuf_tensor(f"{pref}_strip", [1, 3 * TF], F32))
    S_A = strip[0:1, 0:TF]
    S_B = strip[0:1, TF:2 * TF]
    S_C = strip[0:1, 2 * TF:3 * TF]

    pstat = ps_pool[0:1, 0:T]
    sq = [es2.enter_context(nc.sbuf_tensor(f"{pref}_sq{i}", [128, T], BF16)) for i in range(2)]
    for kind in range(2):
        S_T = S_A if kind == 0 else S_B
        for hh in range(2):
            sq_reads = []
            for ii in range(4):
                i = 4 * hh + ii
                if kind == 1:
                    if ii >= 2:
                        ctx.wait("sc", spe, sq_reads[ii - 2])
                    act(ctx, sq[ii % 2][:, 0:T], src_tiles[i][:, 1:T + 1], AF.Square)
                    ctx.wait("pe", ssc)
                if ii == 0:
                    ctx.wait("pe", sv)
                for c in range(NCH):
                    rhs = (srcb_tiles[i][:, col(c)] if kind == 0
                           else sq[ii % 2][:, 512 * c:512 * (c + 1)])
                    mm(ctx, pstat[0:1, 512 * c:512 * (c + 1)], ones_col[:, :], rhs,
                       start=(ii == 0), stop=(ii == 3))
                sq_reads.append(spe.n)
            ctx.wait("v", spe)
            if hh == 0:
                cp(ctx, "v", S_T[:, 1:T + 1], pstat[0:1, 0:T])
                mset(ctx, "v", S_T[:, 0:1], 0.0)
            elif not shifted:
                tt(ctx, "v", S_T[:, 1:T + 1], S_T[:, 1:T + 1], pstat[0:1, 0:T], ALU.add)
            else:
                tt(ctx, "v", S_T[:, 2:T + 1], S_T[:, 2:T + 1], pstat[0:1, 0:T - 1], ALU.add)
    ctx.wait("v", sv)
    ts(ctx, "v", S_A[:, 1:T + 1], S_A[:, 1:T + 1], 1.0 / D, None, ALU.mult)
    mean_done = sv.n
    ts(ctx, "v", S_B[:, 1:T + 1], S_B[:, 1:T + 1], 1.0 / D, None, ALU.mult)
    var_done = sv.n
    ctx.wait("sc", sv, mean_done)
    act(ctx, S_C[:, 1:T + 1], S_A[:, 1:T + 1], AF.Square)
    ctx.wait("v", ssc)
    ctx.wait("v", sv, var_done)
    tt(ctx, "v", S_B[:, 1:T + 1], S_B[:, 1:T + 1], S_C[:, 1:T + 1], ALU.subtract)
    ctx.wait("v", sv)
    ts(ctx, "v", S_B[:, 1:T + 1], S_B[:, 1:T + 1], 1e-5, None, ALU.add)
    ctx.wait("sc", sv)
    act(ctx, S_C[:, 1:T + 1], S_B[:, 1:T + 1], AF.Sqrt)
    ctx.wait("v", ssc)
    recip(ctx, S_B[:, 1:T + 1], S_C[:, 1:T + 1])
    ctx.wait("v", sv)
    tt(ctx, "v", S_C[:, 1:T + 1], S_A[:, 1:T + 1], S_B[:, 1:T + 1], ALU.mult)
    strips_done = sv.n

    if out_tiles is None:
        tmpc = es2.enter_context(nc.sbuf_tensor(f"{pref}_tmpc", [128, 1024], F32))
        mgc = es2.enter_context(nc.sbuf_tensor(f"{pref}_mgc", [128, 1024], F32))
    for hh in range(2):
        ctx.wait("pe", sv, strips_done)
        ctx.wait("pe", sv)
        for c in range(2):
            cc = 2 * hh + c
            mm(ctx, ps_pool[:, 512 * c:512 * (c + 1)], ones128f[:, :], S_B[:, col(cc)])
            mm(ctx, ps_pool[:, 1024 + 512 * c:1024 + 512 * (c + 1)], ones128f[:, :],
               S_C[:, col(cc)])
        ctx.wait("v", spe)
        for i in range(DT):
            for c in range(2):
                cc = 2 * hh + c
                if shifted and i >= 4:
                    s_ap = src_tiles[i][:, col_sh(cc)]
                else:
                    s_ap = src_tiles[i][:, col(cc)]
                pr = ps_pool[:, 512 * c:512 * (c + 1)]
                pm = ps_pool[:, 1024 + 512 * c:1024 + 512 * (c + 1)]
                if out_tiles is not None:
                    stt(ctx, "v", out_tiles[i][:, col(cc)], s_ap, 1.0, pr,
                        ALU.mult, ALU.mult)
                    tt(ctx, "v", out_tiles[i][:, col(cc)], out_tiles[i][:, col(cc)],
                       pm, ALU.subtract)
                else:
                    ga = gamma[:, i:i + 1] if gamma is not None else 1.0
                    tc_ap = tmpc[:, 512 * c:512 * (c + 1)]
                    mg_ap = mgc[:, 512 * c:512 * (c + 1)]
                    stt(ctx, "v", tc_ap, s_ap, ga, pr, ALU.mult, ALU.mult)
                    if gamma is not None:
                        ts(ctx, "v", mg_ap, pm, gamma[:, i:i + 1], beta[:, i:i + 1],
                           ALU.mult, ALU.subtract)
                    else:
                        cp(ctx, "v", mg_ap, pm)
                    ctx.wait("v", sv)
                    tt(ctx, "v", outb_tiles[i][:, col(cc)], tc_ap, mg_ap, ALU.subtract)
    if out_tiles is not None:
        apply_done = ctx.s["v"].n
        for i in range(DT):
            ctx.wait("sc", sv, apply_done)
            if gamma is not None:
                act(ctx, outb_tiles[i][:, 1:T + 1], out_tiles[i][:, 1:T + 1],
                    AF.Identity, bias=beta[:, i:i + 1], scale=gamma[:, i:i + 1])
            else:
                scp(ctx, outb_tiles[i][:, 1:T + 1], out_tiles[i][:, 1:T + 1])
    ctx.barrier()
    es2.close()


def emit_ln0(ctx):
    X = ctx.buf["X"]; Y = ctx.buf["Y"]; XB = ctx.buf["XB"]
    load_stream(ctx, X, ctx.inp["x_fm"])
    load_stream(ctx, XB, ctx.inp["x_fm"], cast=True)
    for t in Y:
        mset(ctx, "gp", t[:, 0:1], 0.0)
        mset(ctx, "gp", t[:, T + 1:TF], 0.0)
    ctx.barrier()
    emit_ln_fm(ctx, X, XB, Y, XB, "ln0")





NAUG = 6
RB_BIAS = 64 + NAUG
KP1 = RB_BIAS
KP2 = RB_BIAS + 1


def emit_attn0(ctx):
    nc = ctx.nc
    io = ctx.inp
    X, XB = ctx.buf["X"], ctx.buf["XB"]
    spe, sv, ssc, sdg, sds = ctx.s["pe"], ctx.s["v"], ctx.s["sc"], ctx.s["dg"], ctx.s["ds"]

    WQTR = [ctx.sb(f"wqtr{i}", [128, 1024], BF16) for i in range(4)]
    WKT = [ctx.sb(f"wkt{i}", [128, 256], BF16) for i in range(4)]
    WVT = [ctx.sb(f"wvt{i}", [128, 256], BF16) for i in range(4)]
    WOTH = ctx.sb("woth", [64, 4 * 1024], BF16)
    QA = [ctx.sb(f"qa{j}", [128, T], BF16) for j in range(4)]
    KA = [ctx.sb(f"ka{g}", [128, T], BF16) for g in range(4)]
    VA = ctx.sb("va", [128, 4 * 16 * 65], BF16)
    AO = [ctx.sb(f"ao{j}", [64, T], BF16) for j in range(4)]
    PP = [ctx.sb(f"pp{i}", [128, 512], BF16) for i in range(4)]
    SQQ = ctx.sb("sqq", [64, T], BF16)
    QNS = ctx.sb("qns", [1, 512], F32)
    MKS = ctx.sb("mks", [1, 8], F32)
    DNR = ctx.sb("dnr", [65, 512], F32)
    RB64 = ctx.sb("rb64", [64, 512], F32)
    ONES65 = ctx.sb("ones65", [65, 64], F32)
    PBIG1 = ctx.psum("pbig1", [128, T], F32)
    PBIG2 = ctx.psum("pbig2", [128, 1024], F32)
    PO = ctx.psum("po", [65, 512], F32)
    PB64 = ctx.psum("pb64", [64, 512], F32)
    D_SC = nc.dram_tensor(ctx.prefix + "d_sc", [16, 8], F32, kind="Internal").ap()
    D_B = nc.dram_tensor(ctx.prefix + "d_b", [2, T], F32, kind="Internal").ap()

    mset(ctx, "gp", ONES65[64:65, :], 1.0)
    for j in range(4):
        mset(ctx, "gp", QA[j][64:128, :], 0.0)
    for g in range(4):
        mset(ctx, "gp", KA[g][64:128, :], 0.0)

    for i in range(4):
        ctx.dma("gp", WKT[i][:, :], io["wk_t"][128 * i:128 * (i + 1), :])
        ctx.dma("gp", WVT[i][:, :], io["wv_t"][128 * i:128 * (i + 1), :])
    for g in range(4):
        for r in range(3):
            ctx.dma("gp", KA[g][64 + 2 * r:65 + 2 * r, :], io["c_negphi"][:, :])
            ctx.dma("gp", KA[g][65 + 2 * r:66 + 2 * r, :], io["c_negplo"][:, :])
        ctx.dma("gp", KA[g][RB_BIAS:RB_BIAS + 1, :], io["c_onesrowf"][:, :])

    with ExitStack() as es2:
        pfx = ctx.prefix
        WQT = [es2.enter_context(nc.sbuf_tensor(f"{pfx}wqt{i}", [128, 1024], BF16)) for i in range(2)]
        WQTS = [es2.enter_context(nc.sbuf_tensor(f"{pfx}wqts{i}", [128, 1024], BF16)) for i in range(2)]
        TMPA = es2.enter_context(nc.sbuf_tensor(pfx + "tmpa", [128, 1024], BF16))
        SS = es2.enter_context(nc.sbuf_tensor(pfx + "ss", [16, 24], F32))
        SSB = es2.enter_context(nc.sbuf_tensor(pfx + "ssb", [16, 4], BF16))
        LC = es2.enter_context(nc.sbuf_tensor(pfx + "lc", [16, 128], BF16))
        LS = es2.enter_context(nc.sbuf_tensor(pfx + "ls", [16, 128], BF16))
        I16 = es2.enter_context(nc.sbuf_tensor(pfx + "i16", [16, 1024], BF16))
        I16S = es2.enter_context(nc.sbuf_tensor(pfx + "i16s", [16, 1024], BF16))
        O16 = es2.enter_context(nc.sbuf_tensor(pfx + "o16", [16, 128], BF16))
        ctx.dma("sy", I16[:, :], io["c_i16"][:, :])
        ctx.dma("sy", I16S[:, :], io["c_i16s"][:, :])
        ctx.dma("sy", O16[:, :], io["c_ones16"][:, :])
        for i in range(2):
            ctx.dma("gp", WQT[i][:, :], io["wq_t"][128 * i:128 * (i + 1), :])
            ctx.dma("gp", WQTS[i][:, :], io["wq_ts"][128 * i:128 * (i + 1), :])
        QPH = SS[:, 0:1]; EXPS = SS[:, 1:2]; C8 = SS[:, 2:3]; S8 = SS[:, 3:4]
        R1 = SS[:, 4:5]; R2 = SS[:, 5:6]; PI2 = SS[:, 6:7]
        SHF = SS[:, 7:8]; SMF = SS[:, 8:9]; SLF = SS[:, 9:10]
        COMP = SS[:, 16:22]
        ctx.dma("sy", QPH, io["qphase"][:, :])
        ctx.dma("sy", EXPS, io["alslope"][:, :])
        mset(ctx, "v", PI2, math.pi / 2)
        ctx.barrier()
        act(ctx, S8, QPH, AF.Sin)
        act(ctx, C8, QPH, AF.Sin, bias=PI2, scale=-1.0)
        act(ctx, EXPS, EXPS, AF.Exp)
        ctx.wait("v", ssc)
        ts(ctx, "v", C8, C8, 0.125, None, ALU.mult)
        ts(ctx, "v", S8, S8, 0.125, None, ALU.mult)
        ctx.wait("sc", sv)
        scp(ctx, SSB[:, 0:1], EXPS)
        ctx.wait("v", ssc)
        cp(ctx, "v", SHF, SSB[:, 0:1])
        ctx.wait("v", sv)
        tt(ctx, "v", R1, EXPS, SHF, ALU.subtract)
        ctx.wait("sc", sv)
        scp(ctx, SSB[:, 1:2], R1)
        ctx.wait("v", ssc)
        cp(ctx, "v", SMF, SSB[:, 1:2])
        ctx.wait("v", sv)
        tt(ctx, "v", R2, R1, SMF, ALU.subtract)
        ctx.wait("sc", sv)
        scp(ctx, SSB[:, 2:3], R2)
        ctx.wait("v", ssc)
        cp(ctx, "v", SLF, SSB[:, 2:3])
        ctx.wait("v", sv)
        for r, s_ap in enumerate([SHF, SMF, SLF]):
            ts(ctx, "v", COMP[:, 2 * r:2 * r + 1], s_ap, 8.0, None, ALU.mult)
            cp(ctx, "v", COMP[:, 2 * r + 1:2 * r + 2], s_ap)
        ctx.wait("sy", sv)
        ctx.dma("sy", D_SC[:, 0:6], COMP)
        ts(ctx, "v", LC[:, :], O16[:, :], C8, None, ALU.mult)
        ts(ctx, "v", LS[:, :], O16[:, :], S8, None, ALU.mult)
        ctx.wait("pe", sv)
        ctx.wait("pe", sdg)
        for half in range(2):
            mm(ctx, PBIG1[:, 512 * half:512 * (half + 1)], LC[:, :],
               I16[:, 512 * half:512 * (half + 1)])
            mm(ctx, PBIG1[:, 1024 + 512 * half:1024 + 512 * (half + 1)], LS[:, :],
               I16S[:, 512 * half:512 * (half + 1)])
        ctx.wait("v", spe)
        for i in range(4):
            s = i % 2
            if i >= 2:
                ctx.wait("gp", sv)
                ctx.dma("gp", WQT[s][:, :], io["wq_t"][128 * i:128 * (i + 1), :])
                ctx.dma("gp", WQTS[s][:, :], io["wq_ts"][128 * i:128 * (i + 1), :])
                ctx.wait("v", sdg)
            tt(ctx, "v", TMPA[:, :], WQT[s][:, :], PBIG1[:, 0:1024], ALU.mult)
            ctx.wait("v", sv)
            tt(ctx, "v", WQTR[i][:, :], WQTS[s][:, :], PBIG1[:, 1024:2048], ALU.mult)
            ctx.wait("v", sv)
            tt(ctx, "v", WQTR[i][:, :], WQTR[i][:, :], TMPA[:, :], ALU.add)
            ctx.wait("v", sv)
        ctx.barrier()

    for g in range(4):
        for c in range(NCH):
            for kd in range(4):
                mm(ctx, PBIG1[0:64, 512 * c:512 * (c + 1)],
                   WKT[kd][:, 64 * g:64 * (g + 1)], XB[4 + kd][:, col(c)],
                   start=(kd == 0), stop=(kd == 3))
            ctx.wait("sc", spe)
            scp(ctx, KA[g][0:64, 512 * c:512 * (c + 1)], PBIG1[0:64, 512 * c:512 * (c + 1)])
        ctx.wait("pe", ssc)
        ctx.wait("sc", ssc)
        act(ctx, SQQ[0:64, 0:T], KA[g][0:64, 0:T], AF.Square)
        ctx.wait("pe", ssc)
        ctx.wait("pe", sv)
        for c in range(NCH):
            ctx.wait("pe", sv)
            mm(ctx, PB64[0:1, 0:512], ctx.buf["c_onescolb"][0:64, :],
               SQQ[0:64, 512 * c:512 * (c + 1)])
            ctx.wait("sc", spe)
            act(ctx, QNS[0:1, 0:512], PB64[0:1, 0:512], AF.Sqrt)
            ctx.wait("v", ssc)
            red(ctx, MKS[0:1, 4 + g:5 + g], QNS[0:1, 0:512], ALU.max)
            ctx.wait("v", sv)
            if c == 0:
                cp(ctx, "v", MKS[0:1, g:g + 1], MKS[0:1, 4 + g:5 + g])
            else:
                tt(ctx, "v", MKS[0:1, g:g + 1], MKS[0:1, g:g + 1],
                   MKS[0:1, 4 + g:5 + g], ALU.max)
            ctx.wait("sc", sv)
    for tt16 in range(16):
        pvs = PBIG2[:, 512 * (tt16 % 2):512 * (tt16 % 2) + 256]
        if tt16 >= 2:
            ctx.wait("pe", sv)
        for kd in range(4):
            mm(ctx, pvs, XB[kd][:, 1 + 128 * tt16:1 + 128 * (tt16 + 1)],
               WVT[kd][:, :], start=(kd == 0), stop=(kd == 3))
        ctx.wait("v", spe)
        for g in range(4):
            cp(ctx, "v", VA[:, (g * 16 + tt16) * 65:(g * 16 + tt16) * 65 + 64],
               pvs[:, 64 * g:64 * (g + 1)])
    for g in range(4):
        for kt in range(16):
            mset(ctx, "gp", VA[:, (g * 16 + kt) * 65 + 64:(g * 16 + kt) * 65 + 65], 1.0)
    ctx.barrier()

    state = {}
    qa_free = [spe.n] * 4
    ao_free = [spe.n] * 4
    pa_free = [ssc.n, ssc.n]
    pb1exp = [("sc", ssc.n)] * 4
    pp_free = [spe.n] * 4
    po_free = sv.n
    wo_free = [sv.n, sv.n]
    wo_w_free = spe.n

    def stageA(h):
        j = h % 4
        g = h // 4
        ctx.wait("sc", spe, qa_free[j])
        for c in range(NCH):
            b = c % 2
            ctx.wait("pe", ssc, pa_free[b])
            for kd in range(4):
                mm(ctx, PBIG2[0:64, 512 * b:512 * b + 512],
                   WQTR[kd][:, 64 * h:64 * (h + 1)], XB[4 + kd][:, col(c)],
                   start=(kd == 0), stop=(kd == 3))
            ctx.wait("sc", spe)
            scp(ctx, QA[j][0:64, 512 * c:512 * (c + 1)], PBIG2[0:64, 512 * b:512 * b + 512])
            pa_free[b] = ssc.n
        proj_done = ssc.n
        ctx.wait("gp", spe, qa_free[j])
        ctx.dma("gp", QA[j][64:64 + NAUG, 0:1],
                D_SC[h:h + 1, 0:6].rearrange("a b -> (a b)").unsqueeze(1), slow=True)
        ctx.wait("v", sdg)
        ctx.wait("v", ssc, proj_done)
        cp(ctx, "v", QA[j][64:64 + NAUG, 0:T],
           QA[j][64:64 + NAUG, 0:1].broadcast_to([NAUG, T]))
        rows_done = sv.n
        ctx.wait("sc", ssc, proj_done)
        act(ctx, SQQ[0:64, 0:T], QA[j][0:64, 0:T], AF.Square)
        sq_done = ssc.n
        for c in range(NCH):
            b = c % 2
            ctx.wait("pe", ssc, sq_done)
            ctx.wait("pe", ctx.s["ds"])
            ctx.wait("pe", ssc, pa_free[b])
            mm(ctx, PBIG2[0:1, 512 * b:512 * b + 512], ctx.buf["c_onescolb"][0:64, :],
               SQQ[0:64, 512 * c:512 * (c + 1)])
            ctx.wait("sc", spe)
            act(ctx, QNS[0:1, 0:512], PBIG2[0:1, 512 * b:512 * b + 512], AF.Sqrt)
            pa_free[b] = ssc.n
            ctx.wait("v", ssc)
            ts(ctx, "v", QNS[0:1, 0:512], QNS[0:1, 0:512], MKS[0:1, g:g + 1], -0.125,
               ALU.mult, ALU.mult)
            ctx.wait("sy", sv)
            if c == 0:
                ctx.wait("sy", sdg, state.get((h - 1, "bias_dma"), 0))
            ctx.dma("sy", D_B[0:1, 512 * c:512 * (c + 1)], QNS[0:1, 0:512])
        ctx.wait("gp", sds)
        ctx.wait("gp", spe, qa_free[j])
        ctx.dma("gp", QA[j][RB_BIAS:RB_BIAS + 1, 0:T], D_B[0:1, :])
        state[(h, "bias_dma")] = sdg.n
        state[(h, "rows_done")] = rows_done

    def stageB(h):
        nonlocal po_free, wo_w_free
        j = h % 4
        g = h // 4
        LOOK = 3
        ctx.wait("pe", sdg, state[(h, "bias_dma")])
        ctx.wait("pe", sv, state[(h, "rows_done")])
        for qc in range(NCH):
            nk = 4 * (qc + 1)
            ctx.wait("pe", sv, po_free)
            exp_cnt = {}

            def emit_sp(kt):
                b = kt % 4
                bank = PBIG1[:, 512 * b:512 * (b + 1)]
                eng_f, n_f = pb1exp[b]
                ctx.wait("pe", ssc if eng_f == "sc" else sv, n_f)
                mm(ctx, bank, KA[g][0:KP2, 128 * kt:128 * (kt + 1)],
                   QA[j][0:KP2, 512 * qc:512 * (qc + 1)])
                mm_done = spe.n
                masked = False
                for m in range(4):
                    qg0 = 512 * qc + 128 * m
                    if 128 * kt > qg0:
                        if not masked:
                            ctx.wait("v", spe, mm_done)
                            masked = True
                        mset(ctx, "v", bank[:, 128 * m:128 * (m + 1)], NEG)
                    elif 128 * kt == qg0:
                        if not masked:
                            ctx.wait("v", spe, mm_done)
                            masked = True
                        tt(ctx, "v", bank[:, 128 * m:128 * (m + 1)],
                           bank[:, 128 * m:128 * (m + 1)], ctx.buf["c_trikq"][:, :], ALU.add)
                if masked:
                    ctx.wait("sc", sv)
                ctx.wait("sc", spe, mm_done)
                ctx.wait("sc", spe, pp_free[b])
                act(ctx, PP[b][:, :], bank, AF.Exp)
                pb1exp[b] = ("sc", ssc.n)
                exp_cnt[kt] = ssc.n

            def emit_pv(kt):
                b = kt % 4
                ctx.wait("pe", ssc, exp_cnt[kt])
                mm(ctx, PO[:, :], VA[:, (g * 16 + kt) * 65:(g * 16 + kt + 1) * 65],
                   PP[b][:, :], start=(kt == 0), stop=(kt == nk - 1))
                pp_free[b] = spe.n

            for kt in range(nk):
                emit_sp(kt)
                if kt >= LOOK:
                    emit_pv(kt - LOOK)
            for kt in range(max(0, nk - LOOK), nk):
                emit_pv(kt)
            ctx.wait("sc", spe)
            act(ctx, DNR[64:65, 0:512], PO[64:65, 0:512], AF.Ln)
            ctx.wait("sc", ssc)
            act(ctx, DNR[64:65, 0:512], DNR[64:65, 0:512], AF.Exp, scale=-1.0)
            ctx.wait("pe", ssc)
            mm(ctx, PB64[:, :], ONES65[64:65, 0:64], DNR[64:65, 0:512])
            ctx.wait("sc", spe)
            scp(ctx, RB64[:, :], PB64[:, :])
            ctx.wait("v", ssc)
            ctx.wait("v", spe, ao_free[j])
            tt(ctx, "v", AO[j][0:64, 512 * qc:512 * (qc + 1)], PO[0:64, 0:512],
               RB64[:, :], ALU.mult)
            po_free = sv.n
        qa_free[j] = spe.n

        if h % 4 == 3:
            ctx.wait("gp", spe, wo_w_free)
            ctx.dma("gp", WOTH[:, :], io["wo_th"][:, 4096 * g:4096 * (g + 1)])
            ctx.wait("pe", sdg)
            ctx.wait("pe", sv, po_free)
            for i in range(DT):
                for c in range(NCH):
                    b = (i * NCH + c) % 2
                    bank = PBIG1[:, 512 * b:512 * (b + 1)]
                    eng_f, n_f = pb1exp[b]
                    ctx.wait("pe", ssc if eng_f == "sc" else sv, n_f)
                    ctx.wait("pe", sv, wo_free[b])
                    for hh in range(4):
                        mm(ctx, bank, WOTH[0:64, 1024 * hh + 128 * i:1024 * hh + 128 * (i + 1)],
                           AO[hh][0:64, 512 * c:512 * (c + 1)],
                           start=(hh == 0), stop=(hh == 3))
                    ctx.wait("v", spe)
                    tt(ctx, "v", X[i][:, col(c)], X[i][:, col(c)], bank, ALU.add)
                    wo_free[b] = sv.n
                    pb1exp[b] = ("v", sv.n)
            wo_w_free = spe.n
            for jj in range(4):
                ao_free[jj] = spe.n

    stageA(0)
    for h in range(1, 16):
        stageA(h)
        stageB(h - 1)
    stageB(15)
    ctx.barrier()



def emit_resid_ln1(ctx):
    X, Y, XB = ctx.buf["X"], ctx.buf["Y"], ctx.buf["XB"]
    sv, ssc = ctx.s["v"], ctx.s["sc"]
    for i in range(DT):
        tt(ctx, "v", X[i][:, 1:T + 1], X[i][:, 1:T + 1], Y[i][:, 1:T + 1], ALU.add)
    ctx.wait("sc", sv)
    for i in range(DT):
        scp(ctx, XB[i][:, 1:T + 1], X[i][:, 1:T + 1])
        mset(ctx, "gp", XB[i][:, 0:1], 0.0)
        mset(ctx, "gp", XB[i][:, T + 1:TF], 0.0)
    ctx.barrier()
    emit_ln_fm(ctx, X, XB, Y, XB, "ln1")



def emit_attn1(ctx):
    nc = ctx.nc
    io = ctx.inp
    X, XB = ctx.buf["X"], ctx.buf["XB"]
    spe, sv, ssc = ctx.s["pe"], ctx.s["v"], ctx.s["sc"]

    TM = [ctx.sb(f"tm{t}", [128, 1028], BF16) for t in range(16)]
    R = [ctx.sb(f"r{t}", [128, T], BF16) for t in range(16)]
    NEGM = ctx.sb("negm", [128, 1], F32)
    DEN = ctx.sb("den", [128, 1], F32)
    RD = ctx.sb("rd", [128, 1], F32)
    OUTF = [ctx.sb(f"outf{i}", [128, 512], F32) for i in range(2)]
    PS_S = ctx.psum("ps_s", [128, T], F32)
    PS_T = ctx.psum("ps_t", [128, 2048], BF16)

    for t in range(16):
        mset(ctx, "gp", TM[t][:, 1024:1028], 0.0)
        mset(ctx, "gp", TM[t][:, 1024:1025], 1.0)
    for t in range(16):
        for d in range(DT):
            half = 1024 * ((t * DT + d) % 2)
            if t * DT + d >= 2:
                ctx.wait("pe", sv, tp_free[(t * DT + d) % 2])
            tr(ctx, PS_T[:, half:half + 128], XB[d][:, 1 + 128 * t:1 + 128 * (t + 1)],
               ctx.buf["c_idb"][:, :])
            ctx.wait("v", spe)
            cp(ctx, "v", TM[t][:, 128 * d:128 * (d + 1)], PS_T[:, half:half + 128])
            if t == 0 and d == 0:
                tp_free = [sv.n, sv.n]
            tp_free[(t * DT + d) % 2] = sv.n
    ctx.barrier()

    for mt in range(16):
        if mt > 0:
            ctx.wait("pe", ssc)
        for c in range(NCH):
            for d in range(DT):
                mm(ctx, PS_S[:, 512 * c:512 * (c + 1)], XB[d][:, 1 + 128 * mt:1 + 128 * (mt + 1)],
                   XB[d][:, col(c)], start=(d == 0), stop=(d == DT - 1))
        ctx.wait("v", spe)
        red(ctx, NEGM[:, :], PS_S[:, 0:T], ALU.max, negate=True)
        ctx.wait("sc", sv)
        act(ctx, R[mt][:, :], PS_S[:, 0:T], AF.Exp, bias=NEGM[:, :], accum=DEN[:, :])
        ctx.wait("sc", ssc)
        act(ctx, RD[:, :], DEN[:, :], AF.Ln)
        ctx.wait("sc", ssc)
        act(ctx, RD[:, :], RD[:, :], AF.Exp, scale=-1.0)
        ctx.wait("v", ssc)
        ts(ctx, "v", R[mt][:, :], R[mt][:, :], RD[:, :], None, ALU.mult)
    ctx.barrier()

    for d in range(DT):
        for c in range(NCH):
            sl = (d * NCH + c) % 2
            if d * NCH + c >= 2:
                ctx.wait("pe", ctx.s["ds"], out_free[sl])
            for mt in range(16):
                mm(ctx, PS_S[:, 512 * sl:512 * (sl + 1)], TM[mt][:, 128 * d:128 * (d + 1)],
                   R[mt][:, 512 * c:512 * (c + 1)], start=(mt == 0), stop=(mt == 15))
            ctx.wait("v", spe)
            tt(ctx, "v", OUTF[sl][:, :], X[d][:, col(c)], PS_S[:, 512 * sl:512 * (sl + 1)],
               ALU.add)
            ctx.wait("sy", sv)
            ctx.dma("sy", io["out_fm"][128 * d:128 * (d + 1), 512 * c:512 * (c + 1)],
                    OUTF[sl][:, :])
            if d == 0 and c == 0:
                out_free = [ctx.s["ds"].n, ctx.s["ds"].n]
            out_free[sl] = ctx.s["ds"].n
    ctx.wait("sy", ctx.s["ds"])
    ctx.barrier()



def emit_moe(ctx):
    nc = ctx.nc
    io = ctx.inp
    X, XB = ctx.buf["X"], ctx.buf["XB"]
    spe, sv, ssc, sdg, sds = ctx.s["pe"], ctx.s["v"], ctx.s["sc"], ctx.s["dg"], ctx.s["ds"]

    GT = ctx.sb("gt", [128, 64], BF16)
    GS = ctx.sb("gs", [128, 8], F32)
    SS8 = ctx.sb("ss8", [128, 8], F32)
    M1 = ctx.sb("m1c", [128, 1], F32)
    M2 = ctx.sb("m2c", [128, 1], F32)
    ES = ctx.sb("es8", [128, 8], F32)
    RD = ctx.sb("rd8", [128, 1], F32)
    NM1 = ctx.sb("nm1", [128, 1], F32)
    WTM = ctx.sb("wtm", [128, 128], F32)
    WTT = ctx.sb("wtt", [128, 128], F32)
    WRE = ctx.sb("wre", [1, T], F32)
    W1E = [ctx.sb(f"w1e{i}", [128, 3072], BF16) for i in range(DT)]
    W2E = [ctx.sb(f"w2e{i}", [128, 1024], BF16) for i in range(12)]
    U6 = [ctx.sb(f"u6{i}", [128, 12 * 512], BF16) for i in range(2)]
    SIL = [ctx.sb(f"sil{i}", [128, 512], BF16) for i in range(2)]
    PH = ctx.psum("ph", [128, 2048], F32)
    PZ = ctx.psum("pz", [128, 1024], F32)
    PW = ctx.psum("pw", [128, 512], F32)
    PR = ctx.psum("pr", [128, 512], F32)
    D_W = nc.dram_tensor(ctx.prefix + "d_w", [128, 128], F32, kind="Internal").ap()

    for i in range(DT):
        ctx.dma("gp", GT[:, 8 * i:8 * (i + 1)], io["gate_t"][128 * i:128 * (i + 1), :])
    ctx.barrier()

    for t in range(16):
        if t >= 1:
            ctx.wait("pe", sv)
        for d in range(DT):
            mm(ctx, PR[:, 0:8], XB[d][:, 1 + 128 * t:1 + 128 * (t + 1)],
               GT[:, 8 * d:8 * (d + 1)], start=(d == 0), stop=(d == DT - 1))
        ctx.wait("v", spe)
        cp(ctx, "v", GS[:, :], PR[:, 0:8])
        ctx.wait("v", sv)
        red(ctx, M1[:, :], GS[:, :], ALU.max)
        red(ctx, NM1[:, :], GS[:, :], ALU.max, negate=True)
        ctx.wait("v", sv)
        ts(ctx, "v", SS8[:, :], GS[:, :], M1[:, :], NEG, ALU.is_equal, ALU.mult)
        ctx.wait("v", sv)
        tt(ctx, "v", SS8[:, :], SS8[:, :], GS[:, :], ALU.add)
        ctx.wait("v", sv)
        red(ctx, M2[:, :], SS8[:, :], ALU.max)
        ctx.wait("sc", sv)
        act(ctx, ES[:, :], GS[:, :], AF.Exp, bias=NM1[:, :], accum=RD[:, :])
        ctx.wait("sc", ssc)
        act(ctx, RD[:, :], RD[:, :], AF.Ln)
        ctx.wait("sc", ssc)
        act(ctx, RD[:, :], RD[:, :], AF.Exp, scale=-1.0)
        ctx.wait("v", ssc)
        ts(ctx, "v", SS8[:, :], GS[:, :], M2[:, :], None, ALU.is_ge)
        ctx.wait("v", sv)
        tt(ctx, "v", SS8[:, :], SS8[:, :], ES[:, :], ALU.mult)
        ctx.wait("v", sv)
        ts(ctx, "v", WTM[:, t:t + 113:16], SS8[:, :], RD[:, :], None, ALU.mult)
    ctx.wait("pe", sv)
    tr(ctx, PR[:, 0:128], WTM[:, :], ctx.buf["c_idf"][:, :])
    ctx.wait("v", spe)
    cp(ctx, "v", WTT[:, :], PR[:, 0:128])
    ctx.wait("sy", sv)
    ctx.dma("sy", D_W[:, :], WTT[:, :])
    ctx.barrier()
    if _DBG_MOE is not None and _DBG_MOE.get("upto") == "routing":
        return

    ph_free = {(p, q): ("v", sv.n) for p in range(2) for q in range(2)}
    sil_free = [sv.n, sv.n]
    pz_free = [sv.n, sv.n]
    w_free = spe.n
    u6_read = [spe.n, spe.n]
    elist = list(range(9)) if _DBG_MOE is None else _DBG_MOE.get("experts", list(range(9)))
    for e in elist:
        shared = (e == 8)
        nj = 12 if shared else 6
        if _DBG_MOE is not None and "nj_cap" in _DBG_MOE:
            nj = min(nj, _DBG_MOE["nj_cap"])
        ctx.wait("gp", spe, w_free)
        for i in range(DT):
            if shared:
                ctx.dma("gp", W1E[i][:, 0:3072], io["ws1p"][128 * i:128 * (i + 1), :])
            else:
                ctx.dma("gp", W1E[i][:, 0:1536], io["w1ep"][e, 128 * i:128 * (i + 1), :])
        for jj in range(nj):
            if shared:
                ctx.dma("gp", W2E[jj][:, :], io["ws2_t"][128 * jj:128 * (jj + 1), :])
            else:
                ctx.dma("gp", W2E[jj][:, :], io["w2e"][e, 128 * jj:128 * (jj + 1), :])
        if not shared:
            ctx.wait("gp", sds)
            ctx.dma("gp", WRE[:, :], D_W.rearrange("(e r) p -> e (r p)", e=8)[e:e + 1, :])
        w_ready = sdg.n
        ctx.wait("pe", sdg, w_ready)
        for tc in range(NCH):
            us = tc % 2
            U = U6[us]
            if not shared:
                ctx.wait("pe", sv)
                mm(ctx, PW[:, :], ctx.buf["c_ones128f"][:, :], WRE[:, 512 * tc:512 * (tc + 1)])
            pw_done = spe.n
            for jj in range(nj):
                sl = jj % 2
                q2 = jj % 2
                for part in range(2):
                    bank = PH[:, 1024 * q2 + 512 * part:1024 * q2 + 512 * (part + 1)]
                    eng_f, n_f = ph_free[(part, q2)]
                    ctx.wait("pe", ssc if eng_f == "sc" else sv, n_f)
                    for kd in range(DT):
                        mm(ctx, bank,
                           W1E[kd][:, 256 * jj + 128 * part:256 * jj + 128 * (part + 1)],
                           XB[kd][:, col(tc)], start=(kd == 0), stop=(kd == DT - 1))
                mm_done = spe.n
                ctx.wait("sc", spe, mm_done)
                ctx.wait("sc", sv, sil_free[sl])
                act(ctx, SIL[sl][:, :], PH[:, 1024 * q2 + 512:1024 * q2 + 1024], AF.Silu)
                ph_free[(1, q2)] = ("sc", ssc.n)
                ctx.wait("v", ssc)
                ctx.wait("v", spe, mm_done)
                ctx.wait("v", spe, u6_read[us])
                stt(ctx, "v", U[:, 512 * jj:512 * (jj + 1)],
                    PH[:, 1024 * q2:1024 * q2 + 512], 1.0,
                    SIL[sl][:, :], ALU.bypass, ALU.mult)
                ph_free[(0, q2)] = ("v", sv.n)
                sil_free[sl] = sv.n
                if not shared:
                    ctx.wait("v", spe, pw_done)
                    ctx.wait("v", sv)
                    tt(ctx, "v", U[:, 512 * jj:512 * (jj + 1)],
                       U[:, 512 * jj:512 * (jj + 1)], PW[:, :], ALU.mult)
            u_done = sv.n
            for i in range(DT):
                half = i % 2
                bank = PZ[:, 512 * half:512 * (half + 1)]
                ctx.wait("pe", sv, pz_free[half])
                ctx.wait("pe", sv, u_done)
                for jj in range(nj):
                    mm(ctx, bank, W2E[jj][:, 128 * i:128 * (i + 1)],
                       U[:, 512 * jj:512 * (jj + 1)], start=(jj == 0), stop=(jj == nj - 1))
                ctx.wait("v", spe)
                tt(ctx, "v", X[i][:, col(tc)], X[i][:, col(tc)], bank, ALU.add)
                pz_free[half] = sv.n
            u6_read[us] = spe.n
        w_free = spe.n
    ctx.barrier()


def emit_ffn_layer(ctx, l):
    nc = ctx.nc
    io = ctx.inp
    Y, XB = ctx.buf["Y"], ctx.buf["XB"]
    spe, sv, ssc, sdg, sds = ctx.s["pe"], ctx.s["v"], ctx.s["sc"], ctx.s["dg"], ctx.s["ds"]

    QS = ctx.sb("qscol", [128, 1], F32)
    KS = ctx.sb("kscol", [128, 1], F32)
    LNG = ctx.sb("lngc", [128, 8], F32)
    LNB = ctx.sb("lnbc", [128, 8], F32)
    B1P = ctx.sb("b1pc", [128, 44], F32)
    B2C = ctx.sb("b2cc", [128, 8], F32)
    PBIG1 = ctx.psum("pbig1", [128, T], F32)
    PBIG2 = ctx.psum("pbig2", [128, 1024], F32)
    PO = ctx.psum("po", [65, 512], F32)
    PB64 = ctx.psum("pb64", [64, 512], F32)
    D_B = nc.dram_tensor(ctx.prefix + "d_b", [1, T], F32, kind="Internal").ap()

    es_att = ExitStack()
    def sba(name, shape, dt=F32):
        return es_att.enter_context(nc.sbuf_tensor(ctx.prefix + name, shape, dt))
    WKVH = [sba(f"wkvh{i}", [128, 8 * 192], BF16) for i in range(2)]
    LQ = [sba(f"lq{i}", [128, T], BF16) for i in range(2)]
    LK = [sba(f"lk{i}", [128, T], BF16) for i in range(2)]
    LV = [sba(f"lv{i}", [128, 16 * 65], BF16) for i in range(2)]
    AOL = [sba(f"aol{j}", [64, T], BF16) for j in range(4)]
    PP2 = [sba(f"pp2{i}", [128, 512], BF16) for i in range(4)]
    QRAW = sba("qraw", [64, T], BF16)
    TA64 = sba("ta64", [64, T], BF16)
    SQT = TA64
    TMP64 = sba("tmp64", [64, T], BF16)
    RNS = sba("rns", [1, 512], F32)
    MKC = sba("mkc", [64, 4], BF16)
    DNR = sba("dnr", [65, 512], F32)
    RB64 = sba("rb64", [64, 512], F32)
    ONES65 = sba("ones65", [65, 64], F32)
    WOL = sba("wol", [64, 4096], BF16)
    ROTQ1 = sba("rotq1", [64, T], BF16)
    ROTQ2 = sba("rotq2", [64, T], BF16)
    ROTK1 = sba("rotk1", [64, T], BF16)
    ROTK2 = sba("rotk2", [64, T], BF16)
    for nm, tgt in [("c_rotq1", ROTQ1), ("c_rotq2", ROTQ2), ("c_rotk1", ROTK1), ("c_rotk2", ROTK2)]:
        ctx.dma("sy", tgt[:, :], io[nm][0:64, :])

    mset(ctx, "gp", ONES65[64:65, :], 1.0)
    for i in range(2):
        mset(ctx, "gp", LQ[i][64:128, :], 0.0)
        mset(ctx, "gp", LK[i][64:128, :], 0.0)
        mset(ctx, "gp", LK[i][64:65, :], 1.0)
        for kt in range(16):
            mset(ctx, "gp", LV[i][:, kt * 65 + 64:kt * 65 + 65], 1.0)
    ctx.dma("sy", QS[:, :], io[f"qs{l}"][:, :])
    ctx.dma("sy", KS[:, :], io[f"ks{l}"][:, :])
    ctx.dma("sy", LNG[:, :], io[f"lng{l}"][:, :])
    ctx.dma("sy", LNB[:, :], io[f"lnb{l}"][:, :])
    ctx.dma("sy", B1P[:, :], io[f"b1p{l}"][:, :])
    ctx.dma("sy", B2C[:, :], io[f"b2c{l}"][:, :])
    for i in range(DT):
        scp(ctx, XB[i][:, 1:T + 1], Y[i][:, 1:T + 1])
        mset(ctx, "gp", XB[i][:, 0:1], 0.0)
        mset(ctx, "gp", XB[i][:, T + 1:TF], 0.0)
    ctx.barrier()

    rotq = (ROTQ1, ROTQ2, QS)
    rotk = (ROTK1, ROTK2, KS)
    spe_, sv_ = spe, sv

    def shifted_rhs(kd, c):
        return XB[kd][:, col(c)] if kd < 4 else XB[kd][:, col_sh(c)]

    state = {}
    qk_free = [spe.n, spe.n]
    ao_free = [spe.n] * 4
    pa_free = [("v", sv.n), ("v", sv.n)]
    pb1exp = [("sc", ssc.n)] * 4
    pp_free = [spe.n] * 4
    po_free = sv.n
    wo_free = [sv.n, sv.n]
    wkv_free = [spe.n, spe.n]

    def prep_qk(s, wslot, off, dest, tabs):
        t1, t2, scol = tabs
        for c in range(NCH):
            b = c % 2
            eng_f, n_f = pa_free[b]
            ctx.wait("pe", ssc if eng_f == "sc" else sv, n_f)
            for kd in range(DT):
                mm(ctx, PBIG2[0:64, 512 * b:512 * b + 512],
                   WKVH[wslot][:, 192 * kd + off:192 * kd + off + 64], shifted_rhs(kd, c),
                   start=(kd == 0), stop=(kd == DT - 1))
            ctx.wait("sc", spe)
            scp(ctx, QRAW[0:64, 512 * c:512 * (c + 1)], PBIG2[0:64, 512 * b:512 * b + 512])
            pa_free[b] = ("sc", ssc.n)
        act(ctx, SQT[0:64, 0:T], QRAW[0:64, 0:T], AF.Square)
        sq_done = ssc.n
        for c in range(NCH):
            b = c % 2
            eng_f, n_f = pa_free[b]
            ctx.wait("pe", ssc if eng_f == "sc" else sv, n_f)
            ctx.wait("pe", ssc, sq_done)
            mm(ctx, PBIG2[0:1, 512 * b:512 * b + 512], ctx.buf["c_onescolb"][0:64, :],
               SQT[0:64, 512 * c:512 * (c + 1)])
            ctx.wait("sc", spe)
            act(ctx, RNS[0:1, 0:512], PBIG2[0:1, 512 * b:512 * b + 512], AF.Sqrt)
            ctx.wait("v", ssc)
            ts(ctx, "v", RNS[0:1, 0:512], RNS[0:1, 0:512], 1e-12, None, ALU.max)
            ctx.wait("v", sv)
            recip(ctx, RNS[0:1, 0:512], RNS[0:1, 0:512])
            ctx.wait("pe", sv)
            mm(ctx, PBIG2[0:64, 512 * b:512 * b + 512], ctx.buf["c_ones128f"][0:1, 0:64],
               RNS[0:1, 0:512])
            ctx.wait("v", spe)
            stt(ctx, "v", TMP64[0:64, 512 * c:512 * (c + 1)],
                QRAW[0:64, 512 * c:512 * (c + 1)], scol[0:64, 0:1],
                PBIG2[0:64, 512 * b:512 * b + 512], ALU.mult, ALU.mult)
            pa_free[b] = ("v", sv.n)
        for c in range(NCH):
            b = c % 2
            eng_f, n_f = pa_free[b]
            ctx.wait("pe", ssc if eng_f == "sc" else sv, n_f)
            mm(ctx, PBIG2[0:64, 512 * b:512 * b + 512], ctx.buf["c_permrot"][0:64, 0:64],
               TMP64[0:64, 512 * c:512 * (c + 1)])
            ctx.wait("v", spe)
            tt(ctx, "v", TA64[0:64, 512 * c:512 * (c + 1)], TMP64[0:64, 512 * c:512 * (c + 1)],
               t1[0:64, 512 * c:512 * (c + 1)], ALU.mult)
            tt(ctx, "v", QRAW[0:64, 512 * c:512 * (c + 1)],
               PBIG2[0:64, 512 * b:512 * b + 512],
               t2[0:64, 512 * c:512 * (c + 1)], ALU.mult)
            ctx.wait("v", sv)
            tt(ctx, "v", dest[0:64, 512 * c:512 * (c + 1)], TA64[0:64, 512 * c:512 * (c + 1)],
               QRAW[0:64, 512 * c:512 * (c + 1)], ALU.add)
            pa_free[b] = ("v", sv.n)

    def stageA(h):
        s = h % 2
        ws = h % 2
        ctx.wait("gp", spe, wkv_free[ws])
        for kd in range(DT):
            ctx.dma("gp", WKVH[ws][:, 192 * kd:192 * (kd + 1)],
                    io[f"wqkv{l}"][128 * kd:128 * (kd + 1), 192 * h:192 * (h + 1)])
        ctx.wait("pe", sdg)
        ctx.wait("pe", spe, qk_free[s])
        ctx.wait("v", spe, qk_free[s])
        prep_qk(s, ws, 0, LQ[s], rotq)
        prep_qk(s, ws, 64, LK[s], rotk)
        prep_done = sv.n
        wkv_left = spe.n
        for t16 in range(16):
            b = t16 % 2
            eng_f, n_f = pa_free[b]
            ctx.wait("pe", ssc if eng_f == "sc" else sv, n_f)
            for kd in range(DT):
                lt = XB[kd][:, 1 + 128 * t16:1 + 128 * (t16 + 1)] if kd < 4 else \
                     XB[kd][:, 128 * t16:128 * (t16 + 1)]
                mm(ctx, PBIG2[:, 512 * b:512 * b + 64], lt,
                   WKVH[ws][:, 192 * kd + 128:192 * kd + 192],
                   start=(kd == 0), stop=(kd == DT - 1))
            ctx.wait("sc", spe)
            scp(ctx, LV[s][:, t16 * 65:t16 * 65 + 64], PBIG2[:, 512 * b:512 * b + 64])
            pa_free[b] = ("sc", ssc.n)
        wkv_free[ws] = spe.n
        ctx.wait("sc", sv, prep_done)
        act(ctx, SQT[0:64, 0:T], LK[s][0:64, 0:T], AF.Abs)
        sqk_done = ssc.n
        ctx.wait("v", ssc, sqk_done)
        ctx.wait("v", spe)
        for w in range(NCH):
            lo = max(0, 512 * (w - 1))
            red(ctx, MKC[0:64, w:w + 1], SQT[0:64, lo:512 * (w + 1)], ALU.max)
        mkc_done = sv.n
        ctx.wait("sc", sv, mkc_done)
        act(ctx, SQT[0:64, 0:T], LQ[s][0:64, 0:T], AF.Abs)
        sqq_done = ssc.n
        for w in range(NCH):
            b = w % 2
            eng_f, n_f = pa_free[b]
            ctx.wait("pe", ssc if eng_f == "sc" else sv, n_f)
            ctx.wait("pe", ssc, sqq_done)
            ctx.wait("pe", ctx.s["ds"])
            mm(ctx, PBIG2[0:1, 512 * b:512 * b + 512], MKC[0:64, w:w + 1],
               SQT[0:64, 512 * w:512 * (w + 1)])
            ctx.wait("v", spe)
            ts(ctx, "v", RNS[0:1, 0:512], PBIG2[0:1, 512 * b:512 * b + 512], -1.0,
               None, ALU.mult)
            pa_free[b] = ("v", sv.n)
            ctx.wait("sy", sv)
            if w == 0:
                ctx.wait("sy", sdg, state.get((h - 1, "bias_dma"), 0))
            ctx.dma("sy", D_B[0:1, 512 * w:512 * (w + 1)], RNS[0:1, 0:512])
        ctx.wait("gp", sds)
        ctx.wait("gp", spe, qk_free[s])
        ctx.dma("gp", LQ[s][64:65, 0:T], D_B[0:1, :])
        state[(h, "bias_dma")] = sdg.n

    def stageB(h):
        nonlocal po_free
        s = h % 2
        j = h % 4
        LOOK = 3
        ctx.wait("pe", sdg, state[(h, "bias_dma")])
        for w in range(NCH):
            kt0 = max(0, 4 * (w - 1))
            nk = 4 * w + 4 - kt0
            ctx.wait("pe", sv, po_free)
            exp_cnt = {}

            def emit_sp(idx, kt):
                b = idx % 4
                bank = PBIG1[:, 512 * b:512 * (b + 1)]
                eng_f, n_f = pb1exp[b]
                ctx.wait("pe", ssc if eng_f == "sc" else sv, n_f)
                mm(ctx, bank, LK[s][0:65, 128 * kt:128 * (kt + 1)],
                   LQ[s][0:65, 512 * w:512 * (w + 1)])
                mm_done = spe.n
                masked = False
                for m in range(4):
                    if kt < 4 * w:
                        a = kt - 4 * (w - 1)
                        kind = "skip" if a > m else ("tri_qk" if a == m else "neg")
                    else:
                        b_ = kt - 4 * w
                        kind = "skip" if b_ < m else ("tri_kq" if b_ == m else "neg")
                    if kind == "skip":
                        continue
                    if not masked:
                        ctx.wait("v", spe, mm_done)
                        masked = True
                    blk = bank[:, 128 * m:128 * (m + 1)]
                    if kind == "neg":
                        mset(ctx, "v", blk, NEG)
                    elif kind == "tri_qk":
                        tt(ctx, "v", blk, blk, ctx.buf["c_triqk"][:, :], ALU.add)
                    else:
                        tt(ctx, "v", blk, blk, ctx.buf["c_trikq"][:, :], ALU.add)
                if masked:
                    ctx.wait("sc", sv)
                ctx.wait("sc", spe, mm_done)
                ctx.wait("sc", spe, pp_free[b])
                act(ctx, PP2[b][:, :], bank, AF.Exp)
                pb1exp[b] = ("sc", ssc.n)
                exp_cnt[idx] = ssc.n

            def emit_pv(idx, kt):
                b = idx % 4
                ctx.wait("pe", ssc, exp_cnt[idx])
                mm(ctx, PO[:, :], LV[s][:, kt * 65:(kt + 1) * 65], PP2[b][:, :],
                   start=(idx == 0), stop=(idx == nk - 1))
                pp_free[b] = spe.n

            kts = list(range(kt0, 4 * w + 4))
            for idx, kt in enumerate(kts):
                emit_sp(idx, kt)
                if idx >= LOOK:
                    emit_pv(idx - LOOK, kts[idx - LOOK])
            for idx in range(max(0, nk - LOOK), nk):
                emit_pv(idx, kts[idx])
            ctx.wait("v", spe)
            recip(ctx, DNR[64:65, 0:512], PO[64:65, 0:512])
            ctx.wait("pe", sv)
            mm(ctx, PB64[:, :], ONES65[64:65, 0:64], DNR[64:65, 0:512])
            ctx.wait("sc", spe)
            scp(ctx, RB64[:, :], PB64[:, :])
            ctx.wait("v", ssc)
            ctx.wait("v", spe, ao_free[j])
            tt(ctx, "v", AOL[j][0:64, 512 * w:512 * (w + 1)], PO[0:64, 0:512],
               RB64[:, :], ALU.mult)
            po_free = sv.n
        qk_free[s] = spe.n

        if _DBG_FFN is not None and _DBG_FFN.get("dump_head") == h:
            ctx.wait("gp", sv)
            ctx.op("gp", lambda e: e.dma_start(out=io["dbg_lq"][:, :], in_=LQ[s][:, :]), 16, ctx.s["out"])
            ctx.op("gp", lambda e: e.dma_start(out=io["dbg_lk"][:, :], in_=LK[s][:, :]), 16, ctx.s["out"])
            ctx.op("gp", lambda e: e.dma_start(out=io["dbg_lv"][:, :], in_=LV[s][:, :]), 16, ctx.s["out"])
            ctx.op("gp", lambda e: e.dma_start(out=io["dbg_ao"][:, :], in_=AOL[j][:, :]), 16, ctx.s["out"])
        if h % 4 == 3:
            g4 = h // 4
            ctx.wait("gp", spe)
            ctx.dma("gp", WOL[:, :], io[f"wothl{l}"][:, 4096 * g4:4096 * (g4 + 1)])
            ctx.wait("pe", sdg)
            ctx.wait("pe", sv, po_free)
            for i in range(DT):
                for c in range(NCH):
                    b = (i * NCH + c) % 2
                    bank = PBIG1[:, 512 * b:512 * (b + 1)]
                    eng_f, n_f = pb1exp[b]
                    ctx.wait("pe", ssc if eng_f == "sc" else sv, n_f)
                    ctx.wait("pe", sv, wo_free[b])
                    for hh in range(4):
                        mm(ctx, bank, WOL[0:64, 1024 * hh + 128 * i:1024 * hh + 128 * (i + 1)],
                           AOL[hh][0:64, 512 * c:512 * (c + 1)],
                           start=(hh == 0), stop=(hh == 3))
                    ctx.wait("v", spe)
                    tt(ctx, "v", Y[i][:, col(c)], Y[i][:, col(c)], bank, ALU.add)
                    wo_free[b] = sv.n
                    pb1exp[b] = ("v", sv.n)
            for jj in range(4):
                ao_free[jj] = spe.n

    stageA(0)
    for h in range(1, 8):
        stageA(h)
        stageB(h - 1)
    stageB(7)
    ctx.barrier()
    es_att.close()
    if _DBG_FFN is not None and _DBG_FFN.get("upto") == "xmid":
        return

    for i in range(DT):
        scp(ctx, XB[i][:, 1:T + 1], Y[i][:, 1:T + 1])
    ctx.barrier()
    emit_ln_fm(ctx, Y, XB, None, XB, f"ln2_{l}", gamma=LNG, beta=LNB, shifted=True, ps_pool=PBIG1)
    if _DBG_FFN is not None and _DBG_FFN.get("upto") == "xn2":
        for i in range(DT):
            a, b = io["dbg_xn2"][128 * i:128 * (i + 1), :], XB[i][:, 1:T + 1]
            ctx.op("gp", lambda e, a=a, b=b: e.dma_start(out=a, in_=b), 16, ctx.s["out"])
        return

    with ExitStack() as es2:
        pfx = ctx.prefix
        TCW = 1024
        NTC = T // TCW
        W1C = [es2.enter_context(nc.sbuf_tensor(f"{pfx}w1c{i}", [128, 2048], BF16)) for i in range(2)]
        W2C = [es2.enter_context(nc.sbuf_tensor(f"{pfx}w2c{i}", [128, 2816], BF16)) for i in range(2)]
        U = es2.enter_context(nc.sbuf_tensor(pfx + "ffu", [128, 22 * TCW], BF16))
        GEL = [es2.enter_context(nc.sbuf_tensor(f"{pfx}gel{i}", [128, TCW], BF16)) for i in range(2)]
        w1_free = [spe.n, spe.n]
        w2_free = [spe.n, spe.n]
        gel_free = [sv.n, sv.n]
        pA_free = ("v", sv.n)
        pB_free = ("sc", ssc.n)
        pc_free = ("v", sv.n)
        u_read = {jj: spe.n for jj in range(22)}
        for tc in range(NTC):
            cs = slice(1 + TCW * tc, 1 + TCW * (tc + 1))
            for jj in range(22):
                sl = jj % 2
                ctx.wait("gp", spe, w1_free[sl])
                ctx.dma("gp", W1C[sl][:, :].rearrange("p (a c) -> p a c", a=8),
                        io[f"w1p{l}"][:, 256 * jj:256 * (jj + 1)]
                        .rearrange("(a p) c -> p a c", p=128))
                w1_ready = sdg.n
                ctx.wait("pe", sdg, w1_ready)
                for part in range(2):
                    eng_f, n_f = (pA_free, pB_free)[part]
                    ctx.wait("pe", ssc if eng_f == "sc" else sv, n_f)
                    for kd in range(DT):
                        for c2 in range(TCW // 512):
                            mm(ctx, PBIG1[:, 1024 * part + 512 * c2:1024 * part + 512 * (c2 + 1)],
                               W1C[sl][:, 256 * kd + 128 * part:256 * kd + 128 * (part + 1)],
                               XB[kd][:, 1 + TCW * tc + 512 * c2:1 + TCW * tc + 512 * (c2 + 1)],
                               start=(kd == 0), stop=(kd == DT - 1))
                mm_done = spe.n
                w1_free[sl] = spe.n
                ctx.wait("sc", spe, mm_done)
                ctx.wait("sc", sv, gel_free[sl])
                act(ctx, GEL[sl][:, :], PBIG1[:, 1024:2048], AF.Gelu,
                    bias=B1P[:, 2 * jj + 1:2 * jj + 2])
                pB_free = ("sc", ssc.n)
                ctx.wait("v", ssc)
                ctx.wait("v", spe, mm_done)
                ctx.wait("v", spe, u_read[jj])
                stt(ctx, "v", U[:, TCW * jj:TCW * (jj + 1)], PBIG1[:, 0:1024],
                    B1P[:, 2 * jj:2 * jj + 1], GEL[sl][:, :], ALU.add, ALU.mult)
                pA_free = ("v", sv.n)
                gel_free[sl] = sv.n
            u_done = sv.n
            for i in range(DT):
                sl = i % 2
                ctx.wait("gp", spe, w2_free[sl])
                for hseg in range(2):
                    ctx.dma("gp", W2C[sl][:, 1408 * hseg:1408 * (hseg + 1)]
                            .rearrange("p (a c) -> p a c", a=11),
                            io[f"w2p{l}"][1408 * hseg:1408 * (hseg + 1), 128 * i:128 * (i + 1)]
                            .rearrange("(a p) c -> p a c", p=128))
                w2_ready = sdg.n
                eng_f, n_f = pc_free
                ctx.wait("pe", ssc if eng_f == "sc" else sv, n_f)
                ctx.wait("pe", sdg, w2_ready)
                ctx.wait("pe", sv, u_done)
                for c2 in range(TCW // 512):
                    for jj in range(22):
                        mm(ctx, PBIG2[:, 512 * c2:512 * (c2 + 1)],
                           W2C[sl][:, 128 * jj:128 * (jj + 1)],
                           U[:, TCW * jj + 512 * c2:TCW * jj + 512 * (c2 + 1)],
                           start=(jj == 0), stop=(jj == 21))
                w2_free[sl] = spe.n
                for jj in range(22):
                    u_read[jj] = spe.n
                ctx.wait("v", spe)
                stt(ctx, "v", Y[i][:, cs], PBIG2[:, 0:TCW], B2C[:, i:i + 1],
                    Y[i][:, cs], ALU.add, ALU.add)
                pc_free = ("v", sv.n)
    ctx.barrier()


_EMITTERS = {}

def _get_emitters():
    if not _EMITTERS:
        _EMITTERS.update({
            "ln0": emit_ln0,
            "attn0": emit_attn0,
            "ffn0a": lambda ctx: emit_ffn_layer(ctx, 0),
            "ffn0b": lambda ctx: emit_ffn_layer(ctx, 1),
            "resid_ln1": emit_resid_ln1,
            "moe": emit_moe,
            "attn1": emit_attn1,
        })
    return _EMITTERS



def _declare_phase_io(nc, phase, weights, consts):
    inp = {}
    names = list(_PHASE_WEIGHTS[phase]) + list(_PHASE_CONSTS[phase])
    for n in names:
        arr = weights[n] if n in weights else consts[n]
        dt = BF16 if arr.dtype == BF else F32
        inp[n] = nc.dram_tensor(n, list(arr.shape), dt, kind="ExternalInput").ap()
    if phase == "ln0":
        inp["x_fm"] = nc.dram_tensor("x_fm", [D, T], F32, kind="ExternalInput").ap()
    reads, writes = _PHASE_STATE[phase]
    for n in reads:
        dt = BF16 if n == "st_xb" else F32
        inp[n] = nc.dram_tensor(n, [D, T], dt, kind="ExternalInput").ap()
    for n in writes:
        dt = BF16 if n == "st_xb" else F32
        inp[n + "_o"] = nc.dram_tensor(n + "_o", [D, T], dt, kind="ExternalOutput").ap()
    if phase == "attn1":
        inp["out_fm"] = nc.dram_tensor("out_fm", [D, T], F32, kind="ExternalOutput").ap()
    if _DBG_FFN is not None and phase.startswith("ffn"):
        for n, shape in [("dbg_lq", [128, T]), ("dbg_lk", [128, T]), ("dbg_lv", [128, 16 * 65]),
                         ("dbg_ao", [64, T]), ("dbg_xn2", [D, T])]:
            inp[n] = nc.dram_tensor(n, shape, F32, kind="ExternalOutput").ap()
    return inp


def build_phase_program(phase, weights, consts):
    nc = bass.Bass("TRN2", target_bir_lowering=False)
    io = _declare_phase_io(nc, phase, weights, consts)
    def _cut(ctx, A, tag):
        if _FUSE_UPTO == tag:
            for i in range(DT):
                a, b = io["out_fm"][128 * i:128 * (i + 1), :], A[i][:, 1:T + 1]
                ctx.op("sy", lambda e, a=a, b=b: e.dma_start(out=a, in_=b), 16, ctx.s["out"])
            ctx.wait("sy", ctx.s["out"])
            ctx.finalize()
            return True
        return False

    with ExitStack() as es:
        ctx = Ctx(nc, es, io)
        for sn in _PHASE_STREAMS[phase]:
            ctx.buf[sn] = alloc_stream(ctx, sn, BF16 if sn == "XB" else F32)
        for n in list(_PHASE_CONSTS[phase]):
            if n in DRAM_ONLY_CONSTS:
                continue
            arr = consts[n]
            dt = BF16 if arr.dtype == BF else F32
            t = ctx.sb("sb_" + n, list(arr.shape), dt)
            ctx.dma("sy", t[:, :], io[n][:, :])
            ctx.buf[n] = t
        reads, writes = _PHASE_STATE[phase]
        smap = {"st_x": "X", "st_y": "Y", "st_xb": "XB"}
        for n in reads:
            load_stream(ctx, ctx.buf[smap[n]], io[n], cast=False)
        ctx.barrier()
        _get_emitters()[phase](ctx)
        ctx.barrier()
        for n in writes:
            store_stream(ctx, ctx.buf[smap[n]], io[n + "_o"])
        ctx.wait("sy", ctx.s["out"])
        ctx.finalize()
    return nc





_FUSE_UPTO = None

_ALL_WEIGHTS = sorted({n for ph in _PHASE_WEIGHTS.values() for n in ph})
_ALL_CONSTS = sorted({n for ph in _PHASE_CONSTS.values() for n in ph})


def build_fused_program(weights, consts):
    nc = bass.Bass("TRN2", target_bir_lowering=False)
    io = {}
    for n in _ALL_WEIGHTS:
        arr = weights[n]
        io[n] = nc.dram_tensor(n, list(arr.shape), F32, kind="ExternalInput").ap()
    for n in _ALL_CONSTS:
        arr = consts[n]
        dt = BF16 if arr.dtype == BF else F32
        io[n] = nc.dram_tensor(n, list(arr.shape), dt, kind="ExternalInput").ap()
    io["x_fm"] = nc.dram_tensor("x_fm", [D, T], F32, kind="ExternalInput").ap()
    io["out_fm"] = nc.dram_tensor("out_fm", [D, T], F32, kind="ExternalOutput").ap()
    D_XB = nc.dram_tensor("sp_xb", [D, T], BF16, kind="Internal").ap()
    D_Y = nc.dram_tensor("sp_y", [D, T], F32, kind="Internal").ap()
    def _cut(ctx, A, tag):
        if _FUSE_UPTO == tag:
            for i in range(DT):
                a, b = io["out_fm"][128 * i:128 * (i + 1), :], A[i][:, 1:T + 1]
                ctx.op("sy", lambda e, a=a, b=b: e.dma_start(out=a, in_=b), 16, ctx.s["out"])
            ctx.wait("sy", ctx.s["out"])
            ctx.finalize()
            return True
        return False

    with ExitStack() as es:
        ctx = Ctx(nc, es, io)
        A = alloc_stream(ctx, "A")
        XB = alloc_stream(ctx, "XB", BF16)
        ctx.buf["X"] = A
        ctx.buf["Y"] = A
        ctx.buf["XB"] = XB
        for n in _ALL_CONSTS:
            if n in DRAM_ONLY_CONSTS:
                continue
            arr = consts[n]
            dt = BF16 if arr.dtype == BF else F32
            t = ctx.sb("c__" + n, list(arr.shape), dt)
            ctx.dma("sy", t[:, :], io[n][:, :])
            ctx.buf[n] = t
        ctx.barrier()

        ctx.prefix = "p0_"
        ctx.push_scope()
        emit_ln0(ctx)
        ctx.pop_scope()
        if _cut(ctx, A, "ln0"):
            return nc

        for i in range(DT):
            ctx.dma("sy", D_XB[128 * i:128 * (i + 1), :], XB[i][:, 1:T + 1])
        ctx.barrier()

        for l in range(2):
            ctx.prefix = f"f{l}_"
            ctx.push_scope()
            emit_ffn_layer(ctx, l)
            ctx.pop_scope()
            if _cut(ctx, A, f"ffn{l}"):
                return nc

        for i in range(DT):
            ctx.dma("sy", D_Y[128 * i:128 * (i + 1), :], A[i][:, 1:T + 1])
        ctx.wait("sy", ctx.s["ds"])
        for i in range(DT):
            ctx.dma("sy", A[i][:, 1:T + 1], io["x_fm"][128 * i:128 * (i + 1), :])
            ctx.dma("sy", XB[i][:, 1:T + 1], D_XB[128 * i:128 * (i + 1), :])
        ctx.barrier()

        ctx.prefix = "a0_"
        ctx.push_scope()
        emit_attn0(ctx)
        ctx.pop_scope()
        if _cut(ctx, A, "attn0"):
            return nc

        ctx.prefix = "r1_"
        ctx.push_scope()
        LAND = [ctx.sb(f"land{i}", [128, T], F32) for i in range(2)]
        land_free = [ctx.s["v"].n] * 2
        for i in range(DT):
            sl = i % 2
            ctx.wait("sy", ctx.s["v"], land_free[sl])
            ctx.dma("sy", LAND[sl][:, :], D_Y[128 * i:128 * (i + 1), :])
            ctx.wait("v", ctx.s["ds"])
            tt(ctx, "v", A[i][:, 1:T + 1], A[i][:, 1:T + 1], LAND[sl][:, :], ALU.add)
            land_free[sl] = ctx.s["v"].n
        ctx.wait("sc", ctx.s["v"])
        for i in range(DT):
            scp(ctx, XB[i][:, 1:T + 1], A[i][:, 1:T + 1])
        ctx.barrier()
        emit_ln_fm(ctx, A, XB, None, XB, "ln1f")
        ctx.pop_scope()
        if _cut(ctx, A, "resid"):
            return nc

        ctx.prefix = "m_"
        ctx.push_scope()
        emit_moe(ctx)
        ctx.pop_scope()
        if _cut(ctx, A, "moe"):
            return nc

        ctx.prefix = "a1_"
        ctx.push_scope()
        emit_attn1(ctx)
        ctx.pop_scope()
        ctx.wait("sy", ctx.s["out"])
        ctx.finalize()
    return nc


def kernel(**inputs):
    consts = _host_consts()
    weights = _host_weights(inputs)
    x = np.asarray(inputs["L0_attn_x"])
    x_fm = [np.ascontiguousarray(x[b].T) for b in range(NC)]

    if MODE == "fused":
        nc = build_fused_program(weights, consts)
        in_maps = []
        for b in range(NC):
            m = {n: weights[n] for n in _ALL_WEIGHTS}
            m.update({n: consts[n] for n in _ALL_CONSTS})
            m["x_fm"] = x_fm[b]
            in_maps.append(m)
        res = run_bass_kernel_spmd(nc, in_maps, core_ids=list(range(NC)))
        out = np.stack([np.asarray(res.results[b]["out_fm"]).T for b in range(NC)])
        return np.ascontiguousarray(out.astype(np.float32))

    state = [dict() for _ in range(NC)]
    out = None
    for phase in PHASES:
        nc = build_phase_program(phase, weights, consts)
        reads, writes = _PHASE_STATE[phase]
        in_maps = []
        for b in range(NC):
            m = {}
            for n in _PHASE_WEIGHTS[phase]:
                m[n] = weights[n]
            for n in _PHASE_CONSTS[phase]:
                m[n] = consts[n]
            if phase == "ln0":
                m["x_fm"] = x_fm[b]
            for n in reads:
                m[n] = state[b][n]
            in_maps.append(m)
        res = run_bass_kernel_spmd(nc, in_maps, core_ids=list(range(NC)))
        for b in range(NC):
            for n in writes:
                state[b][n] = np.asarray(res.results[b][n + "_o"])
        if phase == "attn1":
            out = np.stack([np.asarray(res.results[b]["out_fm"]).T for b in range(NC)])
    return np.ascontiguousarray(out.astype(np.float32))
